# version 07 (final)
# baseline (speedup 1.0000x reference)
import contextlib
import numpy as np
import ml_dtypes
import concourse.bass as bass
import concourse.mybir as mybir
from concourse.bass_utils import run_bass_kernel_spmd

F32 = mybir.dt.float32
BF16 = mybir.dt.bfloat16
ALU = mybir.AluOpType
ACT = mybir.ActivationFunctionType

D = 1024
NKT = 8
HEADS = 16
I_DIM = 2816
B = 2
T_FULL = 8192
NCORES = 8
CHUNK = 128
EPS_LN = 1e-5

I_TILES = []
_pos = 0
for _g in range(3):
    for _kt in range(8):
        I_TILES.append((128 if _g < 2 else 96, _pos))
        _pos += 128 if _g < 2 else 96
assert _pos == I_DIM
N_IT = len(I_TILES)

PAIRSWAP = [i ^ 1 for i in range(32)]


def perm_d():
    p = np.zeros(D, dtype=np.int64)
    for kt in range(8):
        for pp in range(128):
            p[kt * 128 + pp] = 16 * (pp // 2) + 8 * (pp % 2) + kt
    return p


def perm_i():
    p = np.zeros(I_DIM, dtype=np.int64)
    base = [0, 1024, 2048]
    for t, (npart, start) in enumerate(I_TILES):
        g, kt = t // 8, t % 8
        for pp in range(npart):
            p[start + pp] = base[g] + 16 * (pp // 2) + 8 * (pp % 2) + kt
    return p


PERM_D = perm_d()
PERM_I = perm_i()

class Cnt:
    def __init__(self, sem):
        self.sem = sem
        self.v = 0

    def inc(self, inst, n=1):
        inst.then_inc(self.sem, n)
        self.v += n
        return self.v


def yoff(h):
    if h < 7:
        return 65 * h
    if h < 14:
        return 512 + 65 * (h - 7)
    return 1024 + 65 * (h - 14)


def build_program(tcore, debug=False, dbgsel=None, barsel=frozenset()):
    NC = tcore // CHUNK
    NTT = tcore // 512
    nc = bass.Bass("TRN2", target_bir_lowering=False, detect_race_conditions=False)

    xin = nc.dram_tensor("xin", [D, tcore], F32, kind="ExternalInput")
    wq_h = nc.dram_tensor("wq", [D, D], F32, kind="ExternalInput")
    wk_h = nc.dram_tensor("wk", [D, D], F32, kind="ExternalInput")
    wv_h = nc.dram_tensor("wv", [D, D], F32, kind="ExternalInput")
    wo_h = nc.dram_tensor("wo", [D, D], F32, kind="ExternalInput")
    wg_h = nc.dram_tensor("wg", [D, 352], F32, kind="ExternalInput")
    wu_h = nc.dram_tensor("wu", [D, 352], F32, kind="ExternalInput")
    wd_h = nc.dram_tensor("wd", [I_DIM, 128], F32, kind="ExternalInput")
    cvec = nc.dram_tensor("cvec", [1, D], F32, kind="ExternalInput")
    wmod_h = nc.dram_tensor("wmod", [D, 1536], F32, kind="ExternalInput")
    identb_h = nc.dram_tensor("identb", [128, 128], BF16, kind="ExternalInput")
    identf_h = nc.dram_tensor("identf", [128, 128], F32, kind="ExternalInput")
    onesb_h = nc.dram_tensor("onesb", [128, 128], BF16, kind="ExternalInput")
    trilb_h = nc.dram_tensor("trilb", [128, 512], BF16, kind="ExternalInput")
    pmask_h = nc.dram_tensor("pmask", [128, 8], F32, kind="ExternalInput")
    modadd_h = nc.dram_tensor("modadd", [128, 64], F32, kind="ExternalInput")
    modmsk_h = nc.dram_tensor("modmsk", [128, 64], F32, kind="ExternalInput")
    modinv_h = nc.dram_tensor("modinv", [128, 64], F32, kind="ExternalInput")
    out_h = nc.dram_tensor("out", [D, tcore], F32, kind="ExternalOutput")

    mod_in = nc.dram_tensor("mod_in", [1, 1536], F32)
    mod_out = nc.dram_tensor("mod_out", [4, 1536], F32)
    kv_in = nc.dram_tensor("kv_in", [128, 520], F32)
    kv_out = nc.dram_tensor("kv_out", [8 * 128, 520], F32)
    wgq_sh = nc.dram_tensor("wgq_sh", [D, 352], BF16)
    wuq_sh = nc.dram_tensor("wuq_sh", [D, 352], BF16)
    wdq_sh = nc.dram_tensor("wdq_sh", [I_DIM, 128], BF16)
    wgq_ag = nc.dram_tensor("wgq_ag", [8 * D, 352], BF16)
    wuq_ag = nc.dram_tensor("wuq_ag", [8 * D, 352], BF16)
    wdq_ag = nc.dram_tensor("wdq_ag", [8 * I_DIM, 128], BF16)
    wgq_d = nc.dram_tensor("wgq_d", [D, I_DIM], BF16)
    wuq_d = nc.dram_tensor("wuq_d", [D, I_DIM], BF16)
    wdq_d = nc.dram_tensor("wdq_d", [128 * N_IT, D], BF16)

    root = contextlib.ExitStack()

    def SB(stack, name, shape, dt_, side=None):
        if side:
            return stack.enter_context(nc.sbuf_tensor(name, shape, dt_, side=side))
        return stack.enter_context(nc.sbuf_tensor(name, shape, dt_))

    def PSm(name, shape, dt_=F32):
        return root.enter_context(nc.psum_tensor(name, shape, dt_))

    modpp = SB(root, "modpp", [128, 64], F32)
    pmask_s = SB(root, "pmask_s", [128, 8], F32)
    modadd_s = SB(root, "modadd_s", [128, 64], F32)
    modmsk_s = SB(root, "modmsk_s", [128, 64], F32)
    modinv_s = SB(root, "modinv_s", [128, 64], F32)
    identb_s = SB(root, "identb_s", [128, 128], BF16)
    identf_s = SB(root, "identf_s", [128, 128], F32)
    onesb_s = SB(root, "onesb_s", [128, 128], BF16)
    trilb_s = SB(root, "trilb_s", [128, 512], BF16)
    crow = SB(root, "crow", [128, 8], F32)
    csil = SB(root, "csil", [128, 8], BF16)
    sc1 = SB(root, "sc1", [128, 512], F32)
    sc2 = SB(root, "sc2", [128, 512], F32)
    sc3 = SB(root, "sc3", [128, 512], F32)
    sc4 = SB(root, "sc4", [128, 512], F32)
    sbA = SB(root, "sbA", [128, 512], BF16)
    sbB = SB(root, "sbB", [128, 512], BF16)
    sbC = SB(root, "sbC", [128, 512], BF16)
    sbD = SB(root, "sbD", [128, 512], BF16)
    sseed = SB(root, "sseed", [128, 520], F32)
    c3t = SB(root, "c3t", [128, 1], F32)
    zrob = SB(root, "zrob", [128, 512], BF16)

    def SEM(name):
        return Cnt(root.enter_context(nc.semaphore(name)))

    s_hx = SEM("s_hx")
    s_gw = SEM("s_gw")
    s_mm = SEM("s_mm")
    s_ve = SEM("s_ve")
    s_sc = SEM("s_sc")
    s_cc = SEM("s_cc")
    s_go = SEM("s_go")
    s_out = SEM("s_out")
    s_kv = SEM("s_kv")
    s_sev = SEM("s_sev")
    s_sco = SEM("s_sco")
    s_ms = SEM("s_ms")
    s_y = SEM("s_y")
    s_yev = SEM("s_yev")
    s_yt = SEM("s_yt")
    s_yte = SEM("s_yte")

    p_g0 = PSm("p_g0", [128, 512])
    p_g1 = PSm("p_g1", [128, 512])
    p_s = PSm("p_s", [128, 512])
    p_z = PSm("p_z", [128, 8])

    block = root.enter_context(nc.Block())
    s_dbg = SEM("s_dbg")
    dbg_names = []

    def dbg(name, ap_f, sem, val, grp="x"):
        if not debug or (dbgsel is not None and grp not in dbgsel):
            return
        ap = ap_f
        t = nc.dram_tensor("dbg_" + name, list(ap.shape), ap.dtype, kind="ExternalOutput")
        dbg_names.append("dbg_" + name)
        @block.sync
        def _(e):
            e.wait_ge(sem, val)
            s_dbg.inc(e.dma_start(out=t.ap(), in_=ap), 16)



    SEM_BY_NAME = {"s_hx": s_hx, "s_gw": s_gw, "s_mm": s_mm, "s_ve": s_ve,
                   "s_sc": s_sc, "s_cc": s_cc, "s_go": s_go, "s_kv": s_kv,
                   "s_sev": s_sev, "s_sco": s_sco, "s_ms": s_ms, "s_y": s_y,
                   "s_yev": s_yev, "s_yt": s_yt, "s_yte": s_yte}

    def barrier_all(bid):
        engs = ("tensor", "vector", "scalar", "sync", "gpsimd")
        sel = [b for b in barsel if b == bid or b.startswith(bid + ":")]
        if not sel:
            return
        semsel = None
        if any(":" in b for b in sel):
            engs = tuple(sorted({b.split(":")[1] for b in sel if ":" in b}))
            named = {b.split(":")[2] for b in sel if b.count(":") >= 2}
            if named:
                semsel = named
        vals = [(s.sem, s.v) for nm, s in SEM_BY_NAME.items()
                if s.v > 0 and (semsel is None or nm in semsel)]
        for eng in engs:
            @getattr(block, eng)
            def _(e, vals=vals):
                for sem, v in vals:
                    e.wait_ge(sem, v)


    capt_n = [0]
    def capt(tag, gate_sem=None, gate_val=None, src_ap=None):
        if 'CAPT' not in barsel:
            return
        capt_n[0] += 1
        @block.scalar
        def _(e, gate_sem=gate_sem, gate_val=gate_val, src_ap=src_ap):
            if gate_sem is not None:
                e.wait_ge(gate_sem, gate_val)
            if src_ap is None:
                s_sc.inc(e.copy(sc4[:, :], trilb_s[:, :]))
            else:
                col = 0
                for ap in src_ap:
                    w = ap.shape[1]
                    e.copy(sc4[:, col:col + w], ap)
                    col += w
                s_sc.inc(e.copy(sc4[:, col:col + 1], c3t[:, :]))
        capV = s_sc.v
        t = nc.dram_tensor("dbg_tri_" + tag, [128, 512], F32, kind="ExternalOutput")
        dbg_names.append("dbg_tri_" + tag)
        @block.sync
        def _(e, t=t, capV=capV):
            e.wait_ge(s_sc.sem, capV)
            s_dbg.inc(e.dma_start(out=t.ap(), in_=sc4[:, :]), 16)
        dmpV = s_dbg.v
        @block.scalar
        def _(e, dmpV=dmpV):
            e.wait_ge(s_dbg.sem, dmpV)
    pA = contextlib.ExitStack()
    x_f = [SB(pA, f"x{k}", [128, tcore], F32) for k in range(NKT)]
    wmodb = [SB(pA, f"wmodb{k}", [128, 1536], BF16) for k in range(NKT)]
    modrow = SB(pA, "modrow", [128, 1536], F32)
    modppT = SB(pA, "modppT", [48, 128], F32)
    csb = [SB(pA, f"csb{k}", [128, 128], BF16) for k in range(NKT)]
    stat_a = SB(pA, "stat_a", [128, 512], F32)
    stat_b = SB(pA, "stat_b", [128, 512], F32)
    pR1 = contextlib.ExitStack()
    zb = [SB(pR1, f"zb{k}", [128, tcore], BF16, side="right") for k in range(NKT)]

    @block.sync
    def _(e):
        for k in range(NKT):
            s_hx.inc(e.dma_start(out=x_f[k][:, :], in_=xin[k * 128:(k + 1) * 128, :]), 16)
    V_XLOAD = s_hx.v

    @block.gpsimd
    def _(e):
        s_gw.inc(e.dma_start(out=identb_s[:, :], in_=identb_h[:, :]), 16)
        s_gw.inc(e.dma_start(out=identf_s[:, :], in_=identf_h[:, :]), 16)
        s_gw.inc(e.dma_start(out=onesb_s[:, :], in_=onesb_h[:, :]), 16)
        s_gw.inc(e.dma_start(out=trilb_s[:, :], in_=trilb_h[:, :]), 16)
        s_gw.inc(e.dma_start(out=pmask_s[:, :], in_=pmask_h[:, :]), 16)
        s_gw.inc(e.dma_start(out=modadd_s[:, :], in_=modadd_h[:, :]), 16)
        s_gw.inc(e.dma_start(out=modmsk_s[:, :], in_=modmsk_h[:, :]), 16)
        s_gw.inc(e.dma_start(out=modinv_s[:, :], in_=modinv_h[:, :]), 16)
        s_gw.inc(e.dma_start(out=crow[:, :], in_=cvec.ap().rearrange("1 (k p) -> p k", p=128), allow_slow_non_contiguous=True), 16)
        for k in range(NKT):
            s_gw.inc(e.dma_start(out=wmodb[k][:, :], in_=wmod_h[k * 128:(k + 1) * 128, :]), 16)
    V_CONSTS = s_gw.v
    capt('T1consts', s_gw.sem, V_CONSTS)

    @block.scalar
    def _(e):
        e.wait_ge(s_gw.sem, V_CONSTS)
        s_sc.inc(e.activation(csil[:, :], crow[:, :], ACT.Silu))
    V_CSIL = s_sc.v

    @block.vector
    def _(e):
        e.wait_ge(s_sc.sem, V_CSIL)
        for k in range(NKT):
            last = e.tensor_copy(csb[k][:, :], csil[:, k:k + 1].broadcast_to([128, 128]))
        s_ve.inc(last)
    V_CSB = s_ve.v

    mod_mm_marks = []
    _ve_base_mod = s_ve.v
    @block.tensor
    def _(e):
        e.wait_ge(s_ve.sem, V_CSB)
        for j in range(3):
            ps = p_g0 if j % 2 == 0 else p_g1
            if j >= 2:
                e.wait_ge(s_ve.sem, _ve_base_mod + (j - 2) + 1)
            for k in range(NKT):
                mm = e.matmul(ps[:, 0:512], csb[k][:, :], wmodb[k][:, j * 512:(j + 1) * 512],
                              start=(k == 0), stop=(k == NKT - 1))
            mod_mm_marks.append(s_mm.inc(mm))

    @block.vector
    def _(e):
        for j in range(3):
            ps = p_g0 if j % 2 == 0 else p_g1
            e.wait_ge(s_mm.sem, mod_mm_marks[j])
            s_ve.inc(e.tensor_copy(modrow[:, j * 512:(j + 1) * 512], ps[:, 0:512]))
    V_MODROW = s_ve.v

    @block.gpsimd
    def _(e):
        e.wait_ge(s_ve.sem, V_MODROW)
        s_go.inc(e.dma_start(out=mod_in.ap(), in_=modrow[0:1, :]), 16)
        e.wait_ge(s_go.sem, s_go.v)
        s_cc.inc(e.collective_compute(
            "AllGather", ALU.bypass,
            replica_groups=[[0, 1, 2, 3], [4, 5, 6, 7]],
            ins=[mod_in.ap().opt()], outs=[mod_out.ap().opt()]))
        e.wait_ge(s_cc.sem, s_cc.v)
        s_go.inc(e.dma_start(out=modppT[0:48, :],
                             in_=mod_out.ap().rearrange("q j -> (q j)").rearrange("(vk p) -> vk p", p=128)), 16)
    V_MODPP = s_go.v

    @block.tensor
    def _(e):
        e.wait_ge(s_go.sem, V_MODPP)
        e.wait_ge(s_gw.sem, V_CONSTS)
        s_mm.inc(e.transpose(p_g0[:, 0:48], modppT[0:48, :], identf_s[0:48, 0:48]))
    V_MODPP_T = s_mm.v

    @block.vector
    def _(e):
        e.wait_ge(s_mm.sem, V_MODPP_T)
        e.memset(modpp[:, :], 0.0)
        s_ve.inc(e.tensor_copy(modpp[:, 0:48], p_g0[:, 0:48]))
    V_MODPP_C = s_ve.v

    @block.vector
    def _(e):
        e.wait_ge(s_ve.sem, V_MODPP_C)
        e.wait_ge(s_gw.sem, V_CONSTS)
        e.memset(zrob[:, :], 0.0)
        e.memset(c3t[:, :], 3.0625)
        M = modpp[:, 0:64]
        e.tensor_tensor(M, M, modadd_s[:, :], ALU.add)
        e.tensor_scalar(sc1[:, 0:64], M, 0.0, 2.0, ALU.is_ge, ALU.mult)
        e.tensor_scalar(sc1[:, 0:64], sc1[:, 0:64], -1.0, None, ALU.add)
        e.tensor_scalar(sc2[:, 0:64], M, -1.0, None, ALU.mult)
        e.tensor_tensor(sc2[:, 0:64], M, sc2[:, 0:64], ALU.max)
        e.tensor_scalar(sc2[:, 0:64], sc2[:, 0:64], 0.25, None, ALU.max)
        e.tensor_tensor(sc2[:, 0:64], sc1[:, 0:64], sc2[:, 0:64], ALU.mult)
        e.tensor_tensor(sc2[:, 0:64], sc2[:, 0:64], modmsk_s[:, :], ALU.mult)
        e.tensor_tensor(M, M, modinv_s[:, :], ALU.mult)
        s_ve.inc(e.tensor_tensor(M, M, sc2[:, 0:64], ALU.add))
    V_MODFIX = s_ve.v
    dbg("modrow", modrow[0:1, :], s_ve.sem, V_MODROW, grp="a")
    dbg("modpp", modpp[:, :], s_ve.sem, V_MODFIX, grp="a")

    def emit_layernorm(src_tiles, ca1_col, sh_col, dst_tiles, sa_t, sb_t):
        ln_stats_mark = [s_ve.v]
        for tt in range(tcore // 512):
            cs = slice(tt * 512, (tt + 1) * 512)
            for k in range(NKT):
                pair = (sbA, sbB) if k % 2 == 0 else (sbC, sbD)
                @block.scalar
                def _(e, k=k, cs=cs, pair=pair, mmv=s_mm.v):
                    if k >= 2:
                        e.wait_ge(s_mm.sem, mmv)
                    s_sc.inc(e.copy(pair[0][:, :], src_tiles[k][:, cs]))
                    s_sc.inc(e.activation(pair[1][:, :], src_tiles[k][:, cs], ACT.Square))
                @block.tensor
                def _(e, k=k, pair=pair, scv=s_sc.v, svm=ln_stats_mark[0]):
                    e.wait_ge(s_sc.sem, scv)
                    if k == 0:
                        e.wait_ge(s_ve.sem, svm)
                    e.matmul(p_g0[:, 0:512], onesb_s[:, :], pair[0][:, :],
                             start=(k == 0), stop=(k == NKT - 1))
                    s_mm.inc(e.matmul(p_g1[:, 0:512], onesb_s[:, :], pair[1][:, :],
                                      start=(k == 0), stop=(k == NKT - 1)))
            @block.vector
            def _(e, cs=cs, mmv=s_mm.v):
                e.wait_ge(s_mm.sem, mmv)
                e.tensor_scalar(sc1[:, :], p_g0[:, 0:512], 1.0 / D, None, ALU.mult)
                e.tensor_tensor(sc2[:, :], sc1[:, :], sc1[:, :], ALU.mult)
                e.scalar_tensor_tensor(sc2[:, :], p_g1[:, 0:512], 1.0 / D, sc2[:, :],
                                       ALU.mult, ALU.subtract)
                e.tensor_scalar(sc2[:, :], sc2[:, :], EPS_LN, None, ALU.add)
                s_ve.inc(e.reciprocal(sc3[:, :], sc2[:, :]))
                ln_stats_mark[0] = s_ve.v
            @block.scalar
            def _(e, cs=cs, vev=s_ve.v):
                e.wait_ge(s_ve.sem, vev)
                s_sc.inc(e.activation(sa_t[:, 0:512], sc3[:, :], ACT.Sqrt))
            @block.vector
            def _(e, cs=cs, scv=s_sc.v):
                e.wait_ge(s_sc.sem, scv)
                e.scalar_tensor_tensor(sb_t[:, 0:512], sc1[:, :], -1.0, sa_t[:, 0:512],
                                       ALU.mult, ALU.mult)
                for k in range(NKT):
                    e.tensor_scalar(sc3[:, :], sa_t[:, 0:512],
                                    modpp[:, ca1_col + k:ca1_col + k + 1], None, ALU.mult)
                    e.tensor_scalar(sc4[:, :], sb_t[:, 0:512],
                                    modpp[:, ca1_col + k:ca1_col + k + 1],
                                    modpp[:, sh_col + k:sh_col + k + 1], ALU.mult, ALU.add)
                    e.tensor_tensor(sc2[:, :], src_tiles[k][:, cs], sc3[:, :], ALU.mult)
                    last = e.tensor_tensor(dst_tiles[k][:, cs], sc2[:, :], sc4[:, :], ALU.add)
                s_ve.inc(last)

    @block.scalar
    def _(e):
        e.wait_ge(s_ve.sem, V_MODFIX)
        e.wait_ge(s_hx.sem, V_XLOAD)
    @block.tensor
    def _(e):
        e.wait_ge(s_ve.sem, V_MODFIX)
    @block.vector
    def _(e):
        e.wait_ge(s_hx.sem, V_XLOAD)
    emit_layernorm(x_f, 8, 0, zb, stat_a, stat_b)
    V_LN1 = s_ve.v
    capt('T2ln1', s_ve.sem, V_LN1)
    dbg("zb0", zb[0][:, :], s_ve.sem, V_LN1, grp="a")
    pA.close()

    barrier_all('B1')
    pB3 = contextlib.ExitStack()
    phq = [SB(pB3, f"phq{k}", [128, tcore], BF16) for k in range(NKT)]
    phkf = [SB(pB3, f"phkf{k}", [128, tcore], BF16) for k in range(NKT)]
    phkt = [SB(pB3, f"phkt{t}", [128, D], BF16) for t in range(NC)]
    vtm = [SB(pB3, f"vtm{t}", [128, HEADS * 64], BF16) for t in range(NC)]
    scrA = SB(pB3, "scrA", [128, 1040], F32)
    ssb = SB(pB3, "ssb", [128, HEADS * 65], BF16)
    msA = SB(pB3, "msA", [128, 512], BF16)
    msB = SB(pB3, "msB", [128, 512], BF16)
    ytm_b = SB(pB3, "ytm_b", [128, D], BF16)
    rec16 = SB(pB3, "rec16", [128, 16], F32)
    rec16b = SB(pB3, "rec16b", [128, 16], F32)
    ysc = scrA[:, 0:1024]
    agb1 = scrA[:, 0:520]
    agb2 = scrA[:, 520:1040]
    pB2 = contextlib.ExitStack()
    wqb = [SB(pB2, f"wqb{i}", [128, D], BF16) for i in range(NKT)]

    def emit_proj_fm(w_dram, dst_phi):
        @block.gpsimd
        def _(e, vv=s_ve.v):
            e.wait_ge(s_ve.sem, vv)
            for k in range(NKT):
                s_gw.inc(e.dma_start(out=wqb[k][:, :], in_=w_dram[k * 128:(k + 1) * 128, :]), 16)
        wv_done = s_gw.v
        for oc in range(NKT):
            for tt in range(tcore // 512):
                ps = p_g0 if (oc * (tcore // 512) + tt) % 2 == 0 else p_g1
                @block.tensor
                def _(e, oc=oc, tt=tt, ps=ps, wv_done=wv_done, vfree=s_ve.v):
                    e.wait_ge(s_gw.sem, wv_done)
                    e.wait_ge(s_ve.sem, vfree)
                    for k in range(NKT):
                        mm = e.matmul(ps[:, 0:512], wqb[k][:, oc * 128:(oc + 1) * 128],
                                      zb[k][:, tt * 512:(tt + 1) * 512],
                                      start=(k == 0), stop=(k == NKT - 1))
                    s_mm.inc(mm)
                @block.scalar
                def _(e, ps=ps, mmv=s_mm.v, vev=s_ve.v):
                    e.wait_ge(s_mm.sem, mmv)
                    e.wait_ge(s_ve.sem, vev)
                    e.activation(sc1[:, :], ps[:, 0:512], ACT.Relu, scale=-1.0)
                    e.activation(sc2[:, :], sc1[:, :], ACT.Exp, scale=-1.0)
                    s_sc.inc(e.activation(sc3[:, :], ps[:, 0:512], ACT.Relu))
                @block.vector
                def _(e, oc=oc, tt=tt, dst=dst_phi, scv=s_sc.v):
                    e.wait_ge(s_sc.sem, scv)
                    s_ve.inc(e.tensor_tensor(dst[oc][:, tt * 512:(tt + 1) * 512],
                                             sc2[:, :], sc3[:, :], ALU.add))

    emit_proj_fm(wq_h, phq)
    emit_proj_fm(wk_h, phkf)

    @block.gpsimd
    def _(e, vv=s_ve.v):
        e.wait_ge(s_ve.sem, vv)
        for k in range(NKT):
            s_gw.inc(e.dma_start(out=wqb[k][:, :], in_=wv_h[k * 128:(k + 1) * 128, :]), 16)
    V_WV = s_gw.v

    @block.vector
    def _(e):
        for tb in range(NC):
            last = e.memset(vtm[tb][:, :], 0.0)
        s_ve.inc(last)
    V_VINIT = s_ve.v
    dbg("phq0", phq[0][:, :], s_ve.sem, s_ve.v, grp="a")
    dbg("phkf0", phkf[0][:, :], s_ve.sem, s_ve.v, grp="a")

    for tb in range(NC):
        for oh in range(2):
            ps = p_g0 if (tb * 2 + oh) % 2 == 0 else p_g1
            @block.tensor
            def _(e, tb=tb, oh=oh, ps=ps, scv=s_sc.v):
                e.wait_ge(s_gw.sem, V_WV)
                e.wait_ge(s_sc.sem, scv)
                for k in range(NKT):
                    mm = e.matmul(ps[:, 0:512], zb[k][:, tb * 128:(tb + 1) * 128],
                                  wqb[k][:, oh * 512:(oh + 1) * 512],
                                  start=(k == 0), stop=(k == NKT - 1))
                s_mm.inc(mm)
            @block.scalar
            def _(e, tb=tb, oh=oh, ps=ps, mmv=s_mm.v):
                e.wait_ge(s_mm.sem, mmv)
                e.wait_ge(s_ve.sem, V_VINIT)
                s_sc.inc(e.copy(vtm[tb][:, oh * 512:(oh + 1) * 512], ps[:, 0:512]))
    V_VPROJ_SC = s_sc.v
    capt('T3vproj')
    V_VPROJ_MM = s_mm.v
    pR1.close()
    pB2.close()

    p_tb = PSm("p_tb", [128, 1024], BF16)
    for tb in range(NC):
        @block.tensor
        def _(e, tb=tb, vev=s_ve.v, scv=s_sc.v):
            e.wait_ge(s_sc.sem, scv)
            e.wait_ge(s_ve.sem, vev)
            for oc in range(NKT):
                mm = e.transpose(p_tb[:, oc * 128:(oc + 1) * 128],
                                 phkf[oc][:, tb * 128:(tb + 1) * 128], identb_s[:, :])
            s_mm.inc(mm)
        @block.vector
        def _(e, tb=tb, mmv=s_mm.v):
            e.wait_ge(s_mm.sem, mmv)
            s_ve.inc(e.tensor_copy(phkt[tb][:, :], p_tb[:, :]))

    barrier_all('B2')
    dbg("phkt0", phkt[0][:, :], s_ve.sem, s_ve.v, grp="a")
    dbg("vtm0", vtm[0][:, :], s_sc.sem, s_sc.v, grp="a")
    pW = contextlib.ExitStack()
    wqs = [SB(pW, f"wqs{i}", [128, 352], F32) for i in range(8)]
    wqt = [SB(pW, f"wqt{i}", [128, 352], F32) for i in range(6)]
    wqob = SB(pW, "wqob", [128, 352], BF16)

    def emit_wquant(srcs, dsts, ncols):
        npart = srcs[0][0]
        cs = slice(0, ncols)
        @block.gpsimd
        def _(e, gv=s_go.v, mmv=s_mm.v, vev=s_ve.v):
            e.wait_ge(s_go.sem, gv)
            e.wait_ge(s_mm.sem, mmv)
            e.wait_ge(s_ve.sem, vev)
            for i, (np_, src_ap) in enumerate(srcs):
                s_gw.inc(e.dma_start(out=wqs[i][0:np_, cs], in_=src_ap), 16)
        ldv = s_gw.v

        def T(i):
            return wqt[i][0:npart, cs]

        for pair in range(4):
            @block.scalar
            def _(e, pair=pair, vv=s_ve.v):
                e.wait_ge(s_gw.sem, ldv)
                e.wait_ge(s_ve.sem, vv)
                e.activation(T(4), wqs[2 * pair][0:npart, cs], ACT.Abs)
                s_sc.inc(e.activation(T(5), wqs[2 * pair + 1][0:npart, cs], ACT.Abs))
            @block.vector
            def _(e, pair=pair, scv=s_sc.v):
                e.wait_ge(s_sc.sem, scv)
                if pair == 0:
                    s_ve.inc(e.tensor_tensor(T(0), T(4), T(5), ALU.max))
                elif pair == 1:
                    e.tensor_tensor(T(1), T(4), T(5), ALU.max)
                    s_ve.inc(e.tensor_tensor(T(0), T(0), T(1), ALU.max))
                elif pair == 2:
                    s_ve.inc(e.tensor_tensor(T(1), T(4), T(5), ALU.max))
                else:
                    e.tensor_tensor(T(2), T(4), T(5), ALU.max)
                    e.tensor_tensor(T(1), T(1), T(2), ALU.max)
                    e.tensor_tensor(T(0), T(0), T(1), ALU.max)
                    e.stream_shuffle(T(1), T(0), PAIRSWAP)
                    e.tensor_tensor(T(0), T(0), T(1), ALU.max)
                    e.tensor_scalar(T(1), T(0), 1.0 / 6.0, 1e-12, ALU.mult, ALU.max)
                    s_ve.inc(e.reciprocal(T(2), T(1)))
        for i in range(8):
            @block.scalar
            def _(e, i=i, vv=s_ve.v):
                e.wait_ge(s_ve.sem, vv)
                e.activation(T(4), wqs[i][0:npart, cs], ACT.Abs)
                s_sc.inc(e.sign(T(5), wqs[i][0:npart, cs]))
            @block.vector
            def _(e, i=i, scv=s_sc.v, gv=s_go.v):
                e.wait_ge(s_sc.sem, scv)
                e.wait_ge(s_go.sem, gv)
                e.tensor_tensor(T(3), T(4), T(2), ALU.mult)
                e.tensor_scalar(T(4), T(3), 5.0, 2.0, ALU.is_gt, ALU.mult)
                e.scalar_tensor_tensor(T(4), T(3), 3.5, T(4), ALU.is_gt, ALU.add)
                e.scalar_tensor_tensor(T(4), T(3), 2.5, T(4), ALU.is_gt, ALU.add)
                e.tensor_scalar(T(0), T(3), 0.25, None, ALU.is_gt)
                e.scalar_tensor_tensor(T(0), T(3), 0.75, T(0), ALU.is_gt, ALU.add)
                e.scalar_tensor_tensor(T(0), T(3), 1.25, T(0), ALU.is_gt, ALU.add)
                e.scalar_tensor_tensor(T(0), T(3), 1.75, T(0), ALU.is_gt, ALU.add)
                e.scalar_tensor_tensor(T(0), T(0), 0.5, T(4), ALU.mult, ALU.add)
                e.tensor_tensor(T(5), T(5), T(1), ALU.mult)
                s_ve.inc(e.tensor_tensor(wqob[0:npart, cs], T(0), T(5), ALU.mult))
            @block.gpsimd
            def _(e, i=i, vev=s_ve.v, dst=dsts[i]):
                e.wait_ge(s_ve.sem, vev)
                s_go.inc(e.dma_start(out=dst, in_=wqob[0:npart, cs]), 16)

    emit_wquant([(128, wg_h[k * 128:(k + 1) * 128, :]) for k in range(8)],
                [wgq_sh[k * 128:(k + 1) * 128, :] for k in range(8)], 352)
    emit_wquant([(128, wu_h[k * 128:(k + 1) * 128, :]) for k in range(8)],
                [wuq_sh[k * 128:(k + 1) * 128, :] for k in range(8)], 352)
    for g in range(3):
        parts = [I_TILES[g * 8 + kt] for kt in range(8)]
        emit_wquant([(np_, wd_h[start:start + np_, :]) for (np_, start) in parts],
                    [wdq_sh[start:start + np_, :] for (np_, start) in parts], 128)

    @block.vector
    def _(e):
        e.memset(sbB[:, :], 0.0)
        s_ve.inc(e.nop())
    V_ZPAD = s_ve.v

    @block.gpsimd
    def _(e):
        e.wait_ge(s_go.sem, s_go.v)
        s_cc.inc(e.collective_compute("AllGather", ALU.bypass, replica_groups=[list(range(8))],
                                      ins=[wgq_sh.ap().opt()], outs=[wgq_ag.ap().opt()]))
        s_cc.inc(e.collective_compute("AllGather", ALU.bypass, replica_groups=[list(range(8))],
                                      ins=[wuq_sh.ap().opt()], outs=[wuq_ag.ap().opt()]))
        s_cc.inc(e.collective_compute("AllGather", ALU.bypass, replica_groups=[list(range(8))],
                                      ins=[wdq_sh.ap().opt()], outs=[wdq_ag.ap().opt()]))
        e.wait_ge(s_cc.sem, s_cc.v)
        for r in range(8):
            s_go.inc(e.dma_start(out=wgq_d[:, r * 352:(r + 1) * 352],
                                 in_=wgq_ag[r * D:(r + 1) * D, :]), 16)
            s_go.inc(e.dma_start(out=wuq_d[:, r * 352:(r + 1) * 352],
                                 in_=wuq_ag[r * D:(r + 1) * D, :]), 16)
            s_go.inc(e.dma_start(out=wdq_d[0:2048, r * 128:(r + 1) * 128],
                                 in_=wdq_ag[r * I_DIM:r * I_DIM + 2048, :]), 16)
            src_g2 = bass.AP(wdq_ag.ap().tensor, (r * I_DIM + 2048) * 128,
                             [[96 * 128, 8], [128, 96], [1, 128]])
            dst_g2 = bass.AP(wdq_d.ap().tensor, 2048 * D + r * 128,
                             [[128 * D, 8], [D, 96], [1, 128]])
            s_go.inc(e.dma_start(out=dst_g2, in_=src_g2), 16)
        e.wait_ge(s_ve.sem, V_ZPAD)
        for j in range(8):
            for hh in range(2):
                dst_pad = bass.AP(wdq_d.ap().tensor, (2048 + 128 * j + 96) * D + hh * 512,
                                  [[D, 32], [1, 512]])
                s_go.inc(e.dma_start(out=dst_pad, in_=sbB[0:32, 0:512]), 16)
    dbg("wgq_row0", None, None, None) if False else None
    V_WQD = s_go.v
    pW.close()

    capt('T4endW')
    barrier_all('B3')

    @block.tensor
    def _(e, vev=s_ve.v, scv=s_sc.v):
        e.wait_ge(s_ve.sem, vev)
        e.wait_ge(s_sc.sem, scv)
        e.matmul(p_s[:, 0:512], onesb_s[:, :], zrob[:, :], start=True, stop=False, skip_group_check=True)
        e.matmul(p_z[:, 0:8], onesb_s[:, :], zrob[:, 0:8], start=True, stop=False, skip_group_check=True)
        for tb in range(NC):
            for h in range(HEADS):
                hp = h // 2
                prow = slice(0, 64) if h % 2 == 0 else slice(64, 128)
                e.matmul(p_s[prow, hp * 64:(hp + 1) * 64],
                         phkt[tb][:, h * 64:(h + 1) * 64],
                         vtm[tb][:, h * 64:(h + 1) * 64],
                         start=False, stop=(tb == NC - 1 and h >= 14),
                         skip_group_check=True,
                         tile_position=(0, 0) if h % 2 == 0 else (0, 64))
            for hp in range(8):
                mm = e.matmul(p_z[:, hp:hp + 1], phkt[tb][:, hp * 128:(hp + 1) * 128],
                              onesb_s[:, 0:1], start=False, stop=(tb == NC - 1 and hp == 7),
                              skip_group_check=True)
        s_mm.inc(mm)
    V_KV1 = s_mm.v

    @block.vector
    def _(e):
        e.wait_ge(s_mm.sem, V_KV1)
        e.tensor_copy(agb1[:, 0:512], p_s[:, 0:512])
        s_ve.inc(e.tensor_copy(agb1[:, 512:520], p_z[:, 0:8]))
    V_KVEVAC = s_ve.v
    dbg("kvtot", agb1[:, :], s_ve.sem, V_KVEVAC, grp="a")

    @block.gpsimd
    def _(e):
        e.wait_ge(s_ve.sem, V_KVEVAC)
        s_go.inc(e.dma_start(out=kv_in.ap(), in_=agb1[:, :]), 16)
        e.wait_ge(s_go.sem, s_go.v)
        s_cc.inc(e.collective_compute("AllGather", ALU.bypass, replica_groups=[list(range(8))],
                                      ins=[kv_in.ap().opt()], outs=[kv_out.ap().opt()]))
        e.wait_ge(s_cc.sem, s_cc.v)
    V_KVAG = s_cc.v

    @block.vector
    def _(e):
        s_ve.inc(e.memset(sseed[:, :], 0.0))
    V_SS0 = s_ve.v

    for r in range(8):
        buf = agb1 if r % 2 == 0 else agb2
        @block.gpsimd
        def _(e, r=r, buf=buf, vv=s_ve.v):
            e.wait_ge(s_cc.sem, V_KVAG)
            e.wait_ge(s_ve.sem, max(V_SS0, vv))
            s_go.inc(e.dma_start(out=buf[:, :], in_=kv_out[r * 128:(r + 1) * 128, :]), 16)
        @block.vector
        def _(e, r=r, buf=buf, gv=s_go.v):
            e.wait_ge(s_go.sem, gv)
            s_ve.inc(e.scalar_tensor_tensor(sseed[:, :], buf[:, :], pmask_s[:, r:r + 1],
                                            sseed[:, :], ALU.mult, ALU.add))
    V_SSEED = s_ve.v
    capt('T5sseed', s_ve.sem, V_SSEED)
    dbg("sseed", sseed[:, :], s_ve.sem, V_SSEED, grp="a")

    barrier_all('B4')
    @block.vector
    def _(e):
        for _ in range(8):
            e.memset(c3t[:, :], 0.0)
    p_sa = p_g0
    p_sb = p_g1
    p_y = PSm("p_y", [128, 1024])
    p_zy = PSm("p_zy", [128, 16])
    pY = contextlib.ExitStack()
    yT = [SB(pY, f"yT{k}", [128, tcore], BF16, side="right") for k in range(NKT)]
    if debug:
        dbg_py = SB(pB3, "dbg_py_s", [128, 1024], F32)
        dbg_ytm = SB(pB3, "dbg_ytm_s", [128, D], BF16)
        dbg_rec = SB(pB3, "dbg_rec_s", [128, 16], F32)
        dbg_ssb0 = SB(pB3, "dbg_ssb0_s", [128, HEADS * 65], BF16)
        dbg_zy = SB(pB3, "dbg_zy_s", [128, 16], F32)

    @block.vector
    def _(e):
        s_ve.inc(e.memset(ssb[:, :], 0.0))
    V_SSBZ = s_ve.v
    capt('T6ssbz', s_ve.sem, V_SSBZ)
    if 'P2DVENOP' in barsel:
        @block.vector
        def _(e):
            e.memset(c3t[:, :], 0.0)
    if 'P2DVENOP2' in barsel:
        @block.vector
        def _(e):
            e.memset(c3t[:, :], 0.0)
            e.memset(c3t[:, :], 0.0)
    if 'P2DVENOP8' in barsel:
        @block.vector
        def _(e):
            for _ in range(8):
                e.memset(c3t[:, :], 0.0)

    @block.tensor
    def _(e):
        e.wait_ge(s_ve.sem, V_SSEED)
        e.matmul(p_s[:, 0:512], identf_s[:, :], sseed[:, 0:512], start=True, stop=False, skip_group_check=True)
        s_kv.inc(e.matmul(p_z[:, 0:8], identf_s[:, :], sseed[:, 512:520], start=True, stop=False, skip_group_check=True))

    for tb in range(NC):
        barrier_all('BA')
        @block.vector
        def _(e, tb=tb):
            e.wait_ge(s_kv.sem, tb + 1)
            if tb == 0:
                e.wait_ge(s_ve.sem, V_SSBZ)
            sp = p_s[:, :].rearrange("p (h c) -> p h c", c=64)
            dv = ssb[:, :].rearrange("p (hp two c) -> p hp two c", two=2, c=65)
            e.tensor_copy(dv[0:64, :, 0, 0:64], sp[0:64, :, :])
            e.tensor_copy(dv[64:128, :, 1, 0:64], sp[64:128, :, :])
            e.tensor_copy(dv[0:64, :, 0, 64:65], p_z[0:64, :].unsqueeze(2))
            s_sev.inc(e.tensor_copy(dv[64:128, :, 1, 64:65], p_z[64:128, :].unsqueeze(2)))

        if 'CAPA' in barsel and tb == 0:
            @block.scalar
            def _(e):
                e.wait_ge(s_sev.sem, 1)
                e.copy(sc1[:, :], ssb[:, 0:512])
                e.copy(sc2[:, :], ssb[:, 512:1024])
                s_sc.inc(e.copy(sc3[:, 0:16], ssb[:, 1024:1040]))
        if any(k in barsel for k in ('CAPA','CAPB','CAPC','CAPE')) and tb == 0:
            capV = s_sc.v
            cap_list = [("cap1", sc1[:, :]), ("cap2", sc2[:, :]), ("cap3", sc3[:, :])]
            if 'CAPC' in barsel:
                cap_list.append(("cap4", sc4[:, :]))
            for cap_nm, cap_ap in cap_list:
                t = nc.dram_tensor("dbg_" + cap_nm, list(cap_ap.shape), cap_ap.dtype, kind="ExternalOutput")
                dbg_names.append("dbg_" + cap_nm)
                @block.sync
                def _(e, t=t, cap_ap=cap_ap, capV=capV):
                    e.wait_ge(s_sc.sem, capV)
                    s_dbg.inc(e.dma_start(out=t.ap(), in_=cap_ap), 16)
        if tb == 0:
            capt('T7sev', s_sev.sem, 1)
            capt('T8kv', s_kv.sem, 2)
            capt('T9nb', s_sev.sem, 1,
                 src_ap=[onesb_s[:, :], identb_s[:, :], crow[:, :], zrob[:, 0:224]])
        @block.tensor
        def _(e, tb=tb):
            e.wait_ge(s_sev.sem, tb + 1)
            for h in range(HEADS):
                hp = h // 2
                prow = slice(0, 64) if h % 2 == 0 else slice(64, 128)
                e.matmul(p_s[prow, hp * 64:(hp + 1) * 64],
                         phkt[tb][:, h * 64:(h + 1) * 64],
                         vtm[tb][:, h * 64:(h + 1) * 64],
                         start=False, stop=(tb == NC - 1 and h >= 14),
                         skip_group_check=True,
                         tile_position=(0, 0) if h % 2 == 0 else (0, 64))
            for hp in range(8):
                mm = e.matmul(p_z[:, hp:hp + 1], phkt[tb][:, hp * 128:(hp + 1) * 128],
                              onesb_s[:, 0:1], start=False, stop=(tb == NC - 1 and hp == 7),
                              skip_group_check=True)
            s_kv.inc(mm)

        for w in range(2):
            @block.tensor
            def _(e, tb=tb, w=w):
                e.wait_ge(s_ms.sem, 2 * tb + w)
                for j in range(4):
                    oc = 4 * w + j
                    e.matmul(p_sa[:, j * 128:(j + 1) * 128],
                             phkf[oc][0:64, tb * 128:(tb + 1) * 128],
                             phq[oc][0:64, tb * 128:(tb + 1) * 128], start=True, stop=True)
                    mm = e.matmul(p_sb[:, j * 128:(j + 1) * 128],
                                  phkf[oc][64:128, tb * 128:(tb + 1) * 128],
                                  phq[oc][64:128, tb * 128:(tb + 1) * 128], start=True, stop=True)
                s_sco.inc(mm)
            @block.vector
            def _(e, tb=tb, w=w, yv=s_y.v):
                e.wait_ge(s_sco.sem, 2 * tb + w + 1)
                e.wait_ge(s_y.sem, yv)
                e.tensor_tensor(msA[:, :], p_sa[:, 0:512], trilb_s[:, :], ALU.mult)
                s_ms.inc(e.tensor_tensor(msB[:, :], p_sb[:, 0:512], trilb_s[:, :], ALU.mult))
            if 'CAPC' in barsel and tb == 0 and w == 0:
                @block.scalar
                def _(e):
                    e.wait_ge(s_ms.sem, 1)
                    s_sc.inc(e.copy(sc1[:, :], msA[:, :]))
            if 'CAPE' in barsel and tb == 0 and w == 0:
                @block.scalar
                def _(e):
                    e.wait_ge(s_sco.sem, 1)
                    e.copy(sc1[:, :], p_sa[:, 0:512])
                    e.copy(sc2[:, :], p_sb[:, 0:512])
                    s_sc.inc(e.copy(sc3[:, :], trilb_s[:, :]))
            @block.tensor
            def _(e, tb=tb, w=w):
                e.wait_ge(s_ms.sem, 2 * tb + w + 1)
                if w == 0:
                    e.wait_ge(s_yev.sem, tb)
                    for bk in range(2):
                        e.matmul(p_y[:, bk * 512:(bk + 1) * 512], onesb_s[:, :],
                                 zrob[:, :], start=True, stop=False, skip_group_check=True)
                    e.matmul(p_zy[:, 0:16], onesb_s[:, :], zrob[:, 0:16],
                             start=True, stop=False, skip_group_check=True)
                for j in range(4):
                    oc = 4 * w + j
                    for par, msx in ((0, msA), (1, msB)):
                        h = 2 * oc + par
                        e.matmul(p_y[:, h * 64:(h + 1) * 64],
                                 msx[:, j * 128:(j + 1) * 128],
                                 vtm[tb][:, h * 64:(h + 1) * 64],
                                 start=False, stop=False, skip_group_check=True)
                        e.matmul(p_zy[:, h:h + 1],
                                 msx[:, j * 128:(j + 1) * 128],
                                 onesb_s[:, 0:1],
                                 start=False, stop=False, skip_group_check=True)
                        e.matmul(p_y[:, h * 64:(h + 1) * 64],
                                 phq[oc][:, tb * 128:(tb + 1) * 128],
                                 ssb[:, h * 65:h * 65 + 64],
                                 start=False, stop=(h in (7, 15)), skip_group_check=True)
                        mm = e.matmul(p_zy[:, h:h + 1],
                                      phq[oc][:, tb * 128:(tb + 1) * 128],
                                      ssb[:, h * 65 + 64:h * 65 + 65],
                                      start=False, stop=(h == 15), skip_group_check=True)
                s_y.inc(mm)

        @block.vector
        def _(e, tb=tb):
            e.wait_ge(s_y.sem, 2 * (tb + 1))
            e.wait_ge(s_yt.sem, tb)
            if debug and tb == 0 and (dbgsel is None or "b1" in dbgsel):
                e.tensor_copy(dbg_py[:, :], p_y[:, :])
                e.tensor_copy(dbg_ssb0[:, :], ssb[:, :])
                e.tensor_copy(dbg_zy[:, :], p_zy[:, 0:16])
            e.tensor_copy(rec16b[:, :], p_zy[:, 0:16])
            e.tensor_scalar(rec16b[:, :], rec16b[:, :], 1e-6, None, ALU.max)
            e.reciprocal(rec16[:, :], rec16b[:, :])
            e.tensor_copy(ysc[:, :], p_y[:, :])
            for h in range(HEADS):
                last = e.tensor_scalar(
                    ytm_b[:, h * 64:(h + 1) * 64],
                    ysc[:, h * 64:(h + 1) * 64],
                    rec16[:, h:h + 1], None, ALU.mult)
            if debug and tb == 0 and (dbgsel is None or "b1" in dbgsel):
                e.tensor_copy(dbg_ytm[:, :], ytm_b[:, :])
                e.tensor_copy(dbg_rec[:, :], rec16[:, :])
            s_yev.inc(last)

        if 'CAPB' in barsel and tb == 0:
            @block.scalar
            def _(e):
                e.wait_ge(s_yev.sem, 1)
                e.copy(sc1[:, :], ytm_b[:, 0:512])
                e.copy(sc2[:, :], ytm_b[:, 512:1024])
                s_sc.inc(e.copy(sc3[:, 0:16], rec16[:, :]))
        if 'CAPC' in barsel and tb == 0:
            @block.scalar
            def _(e):
                e.wait_ge(s_y.sem, 2)
                e.copy(sc2[:, 0:16], p_zy[:, 0:16])
                e.copy(sc3[:, :], p_y[:, 0:512])
                s_sc.inc(e.copy(sc4[:, :], p_y[:, 512:1024]))
        @block.tensor
        def _(e, tb=tb):
            e.wait_ge(s_yev.sem, tb + 1)
            e.wait_ge(s_yte.sem, tb)
            for oc in range(NKT):
                mm = e.transpose(p_tb[:, oc * 128:(oc + 1) * 128],
                                 ytm_b[:, oc * 128:(oc + 1) * 128], identb_s[:, :])
            s_yt.inc(mm)

        @block.vector
        def _(e, tb=tb):
            e.wait_ge(s_yt.sem, tb + 1)
            for oc in range(NKT):
                mm = e.tensor_copy(yT[oc][:, tb * 128:(tb + 1) * 128], p_tb[:, oc * 128:(oc + 1) * 128])
            s_yte.inc(mm)
    V_ATT = s_yte.v
    if debug:
        dbg("py0", dbg_py[:, :], s_yte.sem, V_ATT, grp="b2")
        dbg("zy0", dbg_zy[:, :], s_yte.sem, V_ATT, grp="b2")
        dbg("ytm0", dbg_ytm[:, :], s_yte.sem, V_ATT, grp="b2")
        dbg("rec0", dbg_rec[:, :], s_yte.sem, V_ATT, grp="b2")
        dbg("ssb0", dbg_ssb0[:, :], s_yte.sem, V_ATT, grp="b2")
    dbg("yT0", yT[0][:, :], s_yte.sem, V_ATT, grp="b2")
    dbg("yT4", yT[4][:, :], s_yte.sem, V_ATT, grp="b2")
    dbg("yT7", yT[7][:, :], s_yte.sem, V_ATT, grp="b2")
    dbg("ytm_last", ytm_b[:, :], s_yte.sem, V_ATT, grp="b2")
    dbg("ssb_last", ssb[:, :], s_yte.sem, V_ATT, grp="b2")
    dbg("rec_last", rec16[:, :], s_yte.sem, V_ATT, grp="b2")
    pB3.close()

    barrier_all('B5')
    pE2 = contextlib.ExitStack()
    x2_f = [SB(pE2, f"x2_{k}", [128, tcore], F32) for k in range(NKT)]
    pE1 = contextlib.ExitStack()
    wob = [SB(pE1, f"wob{i}", [128, D], BF16) for i in range(NKT)]
    xres1 = SB(pE1, "xres1", [128, 512], F32)
    xres2 = SB(pE1, "xres2", [128, 512], F32)

    @block.gpsimd
    def _(e):
        e.wait_ge(s_yte.sem, V_ATT)
        for k in range(NKT):
            s_gw.inc(e.dma_start(out=wob[k][:, :], in_=wo_h[k * 128:(k + 1) * 128, :]), 16)
    V_WO = s_gw.v

    for oc in range(NKT):
        for tt in range(tcore // 512):
            ps = p_g0 if (oc * (tcore // 512) + tt) % 2 == 0 else p_g1
            xr = xres1 if tt % 2 == 0 else xres2
            @block.sync
            def _(e, oc=oc, tt=tt, xr=xr, vv=s_ve.v):
                e.wait_ge(s_yte.sem, V_ATT)
                e.wait_ge(s_ve.sem, vv)
                s_hx.inc(e.dma_start(out=xr[:, :], in_=xin[oc * 128:(oc + 1) * 128,
                                                          tt * 512:(tt + 1) * 512]), 16)
            @block.tensor
            def _(e, oc=oc, tt=tt, ps=ps, vv=s_ve.v):
                e.wait_ge(s_gw.sem, V_WO)
                e.wait_ge(s_ve.sem, max(0, vv - 1))
                for k in range(NKT):
                    mm = e.matmul(ps[:, 0:512], wob[k][:, oc * 128:(oc + 1) * 128],
                                  yT[k][:, tt * 512:(tt + 1) * 512],
                                  start=(k == 0), stop=(k == NKT - 1))
                s_mm.inc(mm)
            @block.vector
            def _(e, oc=oc, tt=tt, ps=ps, xr=xr, mmv=s_mm.v, hv=s_hx.v):
                e.wait_ge(s_mm.sem, mmv)
                e.wait_ge(s_hx.sem, hv)
                s_ve.inc(e.scalar_tensor_tensor(x2_f[oc][:, tt * 512:(tt + 1) * 512],
                                                ps[:, 0:512], modpp[:, 16 + oc:17 + oc],
                                                xr[:, :], ALU.mult, ALU.add))
    V_X2 = s_ve.v
    dbg("x2_0", x2_f[0][:, :], s_ve.sem, V_X2, grp="c")
    dbg("x2_1", x2_f[1][:, :], s_ve.sem, V_X2, grp="c")
    dbg("x2_7", x2_f[7][:, :], s_ve.sem, V_X2, grp="c")
    dbg("xres1", xres1[:, :], s_ve.sem, V_X2, grp="c")
    pE1.close()
    pY.close()

    barrier_all('B6')
    pF1 = contextlib.ExitStack()
    h2_f = [SB(pF1, f"h2_{k}", [128, tcore], F32) for k in range(NKT)]
    stat_a2 = SB(pF1, "stat_a2", [128, tcore], F32)
    stat_b2 = SB(pF1, "stat_b2", [128, tcore], F32)

    @block.scalar
    def _(e):
        e.wait_ge(s_ve.sem, V_X2)
    @block.tensor
    def _(e):
        e.wait_ge(s_ve.sem, V_X2)
    emit_layernorm(x2_f, 32, 24, h2_f, stat_a2, stat_b2)
    V_LN2 = s_ve.v
    dbg("h2_0", h2_f[0][:, :], s_ve.sem, V_LN2, grp="c")

    pR3 = contextlib.ExitStack()
    hq2 = [SB(pR3, f"hq2_{k}", [128, tcore], BF16, side="right") for k in range(NKT)]
    HW_ = tcore // 2
    scP = SB(pF1, "scP", [128, HW_], F32)
    rP = SB(pF1, "rP", [128, HW_], F32)

    for half in range(2):
        hs = slice(half * HW_, (half + 1) * HW_)
        qt1 = stat_a2[:, hs]
        qt2 = stat_b2[:, hs]
        for pair in range(4):
            @block.scalar
            def _(e, pair=pair, hs=hs, qt1=qt1, qt2=qt2, vv=s_ve.v, v0=V_LN2):
                e.wait_ge(s_ve.sem, max(v0, vv))
                e.activation(qt1, h2_f[2 * pair][:, hs], ACT.Abs)
                s_sc.inc(e.activation(qt2, h2_f[2 * pair + 1][:, hs], ACT.Abs))
            @block.vector
            def _(e, pair=pair, hs=hs, qt1=qt1, qt2=qt2, scv=s_sc.v):
                e.wait_ge(s_sc.sem, scv)
                if pair == 0:
                    s_ve.inc(e.tensor_tensor(scP[:, :], qt1, qt2, ALU.max))
                elif pair == 1:
                    e.tensor_tensor(rP[:, :], qt1, qt2, ALU.max)
                    s_ve.inc(e.tensor_tensor(scP[:, :], scP[:, :], rP[:, :], ALU.max))
                elif pair == 2:
                    s_ve.inc(e.tensor_tensor(rP[:, :], qt1, qt2, ALU.max))
                else:
                    e.tensor_tensor(qt1, qt1, qt2, ALU.max)
                    e.tensor_tensor(rP[:, :], rP[:, :], qt1, ALU.max)
                    e.tensor_tensor(scP[:, :], scP[:, :], rP[:, :], ALU.max)
                    e.stream_shuffle(rP[:, :], scP[:, :], PAIRSWAP)
                    e.tensor_tensor(scP[:, :], scP[:, :], rP[:, :], ALU.max)
                    e.tensor_scalar(scP[:, :], scP[:, :], 1.0 / 6.0, 1e-12, ALU.mult, ALU.max)
                    s_ve.inc(e.reciprocal(rP[:, :], scP[:, :]))

        for k in range(NKT):
            @block.scalar
            def _(e, k=k, hs=hs, qt1=qt1, vv=s_ve.v):
                e.wait_ge(s_ve.sem, vv)
                s_sc.inc(e.activation(qt1, h2_f[k][:, hs], ACT.Abs))
            @block.vector
            def _(e, k=k, hs=hs, qt1=qt1, qt2=qt2, scv=s_sc.v):
                e.wait_ge(s_sc.sem, scv)
                e.tensor_tensor(qt1, qt1, rP[:, :], ALU.mult)
                e.tensor_scalar(qt2, qt1, 5.0, 2.0, ALU.is_gt, ALU.mult)
                e.scalar_tensor_tensor(qt2, qt1, 3.5, qt2, ALU.is_gt, ALU.add)
                e.scalar_tensor_tensor(qt2, qt1, 2.5, qt2, ALU.is_gt, ALU.add)
                e.tensor_scalar(hq2[k][:, hs], qt1, 0.25, None, ALU.is_gt)
                e.scalar_tensor_tensor(hq2[k][:, hs], qt1, 0.75, hq2[k][:, hs], ALU.is_gt, ALU.add)
                e.scalar_tensor_tensor(hq2[k][:, hs], qt1, 1.25, hq2[k][:, hs], ALU.is_gt, ALU.add)
                e.scalar_tensor_tensor(hq2[k][:, hs], qt1, 1.75, hq2[k][:, hs], ALU.is_gt, ALU.add)
                s_ve.inc(e.scalar_tensor_tensor(qt2, hq2[k][:, hs], 0.5, qt2, ALU.mult, ALU.add))
            @block.scalar
            def _(e, k=k, hs=hs, qt1=qt1, vv=s_ve.v):
                e.wait_ge(s_ve.sem, vv)
                s_sc.inc(e.sign(qt1, h2_f[k][:, hs]))
            @block.vector
            def _(e, k=k, hs=hs, qt1=qt1, qt2=qt2, scv=s_sc.v):
                e.wait_ge(s_sc.sem, scv)
                e.tensor_tensor(qt1, qt1, scP[:, :], ALU.mult)
                s_ve.inc(e.tensor_tensor(hq2[k][:, hs], qt2, qt1, ALU.mult))
    V_HQ2 = s_ve.v
    dbg("hq2_0", hq2[0][:, :], s_ve.sem, V_HQ2, grp="c")

    barrier_all('B7')
    V_HQ2_BAR = V_HQ2
    pF1.close()
    FT = 512
    pF2 = contextlib.ExitStack()
    qmid = [SB(pF2, f"qmid{t}", [128, FT], BF16) for t in range(N_IT)]
    fsil = SB(pF2, "fsil", [128, FT], F32)
    fG = SB(pF2, "fG", [128, FT], F32)
    fmid = [SB(pF2, f"fmid{i}", [128, FT], F32) for i in range(8)]
    fsc = SB(pF2, "fsc", [128, FT], F32)
    frec = SB(pF2, "frec", [128, FT], F32)
    ft1 = SB(pF2, "ft1", [128, FT], F32)
    ft3 = SB(pF2, "ft3", [128, FT], F32)
    wfb1 = SB(pF2, "wfb1", [128, 1024], BF16)
    wfb2 = SB(pF2, "wfb2", [128, 1024], BF16)
    wdb1 = SB(pF2, "wdb1", [128, 256], BF16)
    wdb2 = SB(pF2, "wdb2", [128, 256], BF16)
    outf1 = SB(pF2, "outf1", [128, 512], F32)
    outf2 = SB(pF2, "outf2", [128, 512], F32)

    @block.vector
    def _(e):
        for it in range(16, 24):
            last = e.memset(qmid[it][96:128, :], 0.0)
        s_ve.inc(last)
    V_QMIDZ = s_ve.v

    for tt in range(NTT):
        cs = slice(tt * FT, (tt + 1) * FT)
        for g in range(3):
            for kt in range(8):
                it = g * 8 + kt
                npart, istart = I_TILES[it]
                @block.sync
                def _(e, it=it, npart=npart, istart=istart, vv=s_ve.v, scv=s_sc.v):
                    e.wait_ge(s_go.sem, V_WQD)
                    e.wait_ge(s_ve.sem, vv)
                    s_hx.inc(e.dma_start(
                        out=wfb1[:, 0:8 * npart].rearrange("p (k c) -> p k c", k=8),
                        in_=wgq_d[:, istart:istart + npart].rearrange("(k p) c -> p k c", p=128)), 16)
                    s_hx.inc(e.dma_start(
                        out=wfb2[:, 0:8 * npart].rearrange("p (k c) -> p k c", k=8),
                        in_=wuq_d[:, istart:istart + npart].rearrange("(k p) c -> p k c", p=128)), 16)
                @block.tensor
                def _(e, it=it, npart=npart, hv=s_hx.v, vv=s_ve.v, scv=s_sc.v):
                    e.wait_ge(s_hx.sem, hv)
                    e.wait_ge(s_ve.sem, vv)
                    e.wait_ge(s_sc.sem, scv)
                    for k in range(NKT):
                        mm = e.matmul(p_g0[0:npart, 0:512], wfb1[:, k * npart:(k + 1) * npart],
                                      hq2[k][:, cs], start=(k == 0), stop=(k == NKT - 1))
                    s_mm.inc(mm)
                    for k in range(NKT):
                        mm = e.matmul(p_g1[0:npart, 0:512], wfb2[:, k * npart:(k + 1) * npart],
                                      hq2[k][:, cs], start=(k == 0), stop=(k == NKT - 1))
                    s_mm.inc(mm)
                @block.scalar
                def _(e, npart=npart, mmv=s_mm.v):
                    e.wait_ge(s_mm.sem, mmv - 1)
                    s_sc.inc(e.activation(fsil[0:npart, :], p_g0[0:npart, 0:512], ACT.Silu))
                @block.vector
                def _(e, kt=kt, npart=npart, mmv=s_mm.v, scv=s_sc.v):
                    e.wait_ge(s_mm.sem, mmv)
                    e.wait_ge(s_sc.sem, scv)
                    s_ve.inc(e.tensor_tensor(fmid[kt][0:npart, :], p_g1[0:npart, 0:512],
                                             fsil[0:npart, :], ALU.mult))
            npart = I_TILES[g * 8][0]
            def F(t):
                return t[0:npart, :]
            for pair in range(4):
                @block.scalar
                def _(e, pair=pair, npart=npart, vv=s_ve.v):
                    e.wait_ge(s_ve.sem, vv)
                    e.activation(F(ft1), F(fmid[2 * pair]), ACT.Abs)
                    s_sc.inc(e.activation(F(ft3), F(fmid[2 * pair + 1]), ACT.Abs))
                @block.vector
                def _(e, pair=pair, npart=npart, scv=s_sc.v):
                    e.wait_ge(s_sc.sem, scv)
                    if pair == 0:
                        s_ve.inc(e.tensor_tensor(F(fsc), F(ft1), F(ft3), ALU.max))
                    elif pair == 1:
                        e.tensor_tensor(F(frec), F(ft1), F(ft3), ALU.max)
                        s_ve.inc(e.tensor_tensor(F(fsc), F(fsc), F(frec), ALU.max))
                    elif pair == 2:
                        s_ve.inc(e.tensor_tensor(F(frec), F(ft1), F(ft3), ALU.max))
                    else:
                        e.tensor_tensor(F(ft1), F(ft1), F(ft3), ALU.max)
                        e.tensor_tensor(F(frec), F(frec), F(ft1), ALU.max)
                        e.tensor_tensor(F(fsc), F(fsc), F(frec), ALU.max)
                        e.stream_shuffle(F(frec), F(fsc), PAIRSWAP)
                        e.tensor_tensor(F(fsc), F(fsc), F(frec), ALU.max)
                        e.tensor_scalar(F(fsc), F(fsc), 1.0 / 6.0, 1e-12, ALU.mult, ALU.max)
                        s_ve.inc(e.reciprocal(F(frec), F(fsc)))
            for kt in range(8):
                it = g * 8 + kt
                @block.scalar
                def _(e, kt=kt, npart=npart, vv=s_ve.v):
                    e.wait_ge(s_ve.sem, vv)
                    e.activation(F(ft1), F(fmid[kt]), ACT.Abs)
                    s_sc.inc(e.sign(F(fG), F(fmid[kt])))
                @block.vector
                def _(e, it=it, kt=kt, npart=npart, scv=s_sc.v):
                    e.wait_ge(s_sc.sem, scv)
                    e.tensor_tensor(F(ft1), F(ft1), F(frec), ALU.mult)
                    e.tensor_scalar(F(ft3), F(ft1), 5.0, 2.0, ALU.is_gt, ALU.mult)
                    e.scalar_tensor_tensor(F(ft3), F(ft1), 3.5, F(ft3), ALU.is_gt, ALU.add)
                    e.scalar_tensor_tensor(F(ft3), F(ft1), 2.5, F(ft3), ALU.is_gt, ALU.add)
                    e.tensor_scalar(F(fsil), F(ft1), 0.25, None, ALU.is_gt)
                    e.scalar_tensor_tensor(F(fsil), F(ft1), 0.75, F(fsil), ALU.is_gt, ALU.add)
                    e.scalar_tensor_tensor(F(fsil), F(ft1), 1.25, F(fsil), ALU.is_gt, ALU.add)
                    e.scalar_tensor_tensor(F(fsil), F(ft1), 1.75, F(fsil), ALU.is_gt, ALU.add)
                    e.scalar_tensor_tensor(F(ft3), F(fsil), 0.5, F(ft3), ALU.mult, ALU.add)
                    e.tensor_tensor(F(fG), F(fG), F(fsc), ALU.mult)
                    s_ve.inc(e.tensor_tensor(qmid[it][0:npart, :], F(ft3), F(fG), ALU.mult))
        V_QMID = s_ve.v

        for oc2 in range(4):
            dn_marks = []
            for it in range(N_IT):
                wdb = wdb1 if it % 2 == 0 else wdb2
                @block.sync
                def _(e, it=it, oc2=oc2, wdb=wdb, mmv=s_mm.v):
                    e.wait_ge(s_ve.sem, V_QMID)
                    e.wait_ge(s_mm.sem, mmv)
                    s_hx.inc(e.dma_start(out=wdb[:, :],
                                         in_=wdq_d[it * 128:(it + 1) * 128,
                                                   oc2 * 256:(oc2 + 1) * 256]), 16)
                @block.tensor
                def _(e, it=it, wdb=wdb, hv=s_hx.v, vv=s_ve.v):
                    e.wait_ge(s_hx.sem, hv)
                    if it == 0:
                        e.wait_ge(s_ve.sem, vv)
                    e.matmul(p_g0[:, 0:512], wdb[:, 0:128], qmid[it][:, :],
                             start=(it == 0), stop=(it == N_IT - 1))
                    s_mm.inc(e.matmul(p_g1[:, 0:512], wdb[:, 128:256], qmid[it][:, :],
                                      start=(it == 0), stop=(it == N_IT - 1)))
            V_DN = s_mm.v
            for half in range(2):
                oc = oc2 * 2 + half
                ps = p_g0 if half == 0 else p_g1
                ouf = outf1 if half == 0 else outf2
                @block.vector
                def _(e, oc=oc, ps=ps, ouf=ouf, tt=tt, ov=s_out.v):
                    e.wait_ge(s_mm.sem, V_DN)
                    e.wait_ge(s_out.sem, ov)
                    s_ve.inc(e.scalar_tensor_tensor(ouf[:, :], ps[:, 0:512],
                                                    modpp[:, 40 + oc:41 + oc],
                                                    x2_f[oc][:, cs], ALU.mult, ALU.add))
                @block.sync
                def _(e, oc=oc, ouf=ouf, tt=tt, vv=s_ve.v):
                    e.wait_ge(s_ve.sem, vv)
                    s_out.inc(e.dma_start(out=out_h[oc * 128:(oc + 1) * 128, tt * FT:(tt + 1) * FT],
                                          in_=ouf[:, :]), 16)

    V_OUT = s_out.v

    canary_t = nc.dram_tensor("canary", [128, 1280], BF16, kind="ExternalOutput")
    @block.sync
    def _(e):
        e.wait_ge(s_out.sem, V_OUT)
        s_dbg.inc(e.dma_start(out=canary_t.ap()[:, 0:128], in_=identb_s[:, :]), 16)
        s_dbg.inc(e.dma_start(out=canary_t.ap()[:, 128:256], in_=onesb_s[:, :]), 16)
        s_dbg.inc(e.dma_start(out=canary_t.ap()[:, 256:768], in_=trilb_s[:, :]), 16)
        s_dbg.inc(e.dma_start(out=canary_t.ap()[:, 768:1280], in_=zrob[:, :]), 16)
        e.wait_ge(s_dbg.sem, s_dbg.v)

    if 'POSTDUMP' in barsel:
        pd_list = [("post_sseed", sseed[:, :]), ("post_modpp", modpp[:, :]),
                   ("post_pmask", pmask_s[:, :])]
        for pd_name, pd_ap in pd_list:
            t = nc.dram_tensor("dbg_" + pd_name, list(pd_ap.shape), pd_ap.dtype, kind="ExternalOutput")
            dbg_names.append("dbg_" + pd_name)
            @block.sync
            def _(e, t=t, pd_ap=pd_ap):
                e.wait_ge(s_out.sem, V_OUT)
                s_dbg.inc(e.dma_start(out=t.ap(), in_=pd_ap), 16)
        @block.sync
        def _(e):
            e.wait_ge(s_dbg.sem, s_dbg.v)
    @block.sync
    def _(e):
        e.wait_ge(s_out.sem, V_OUT)
        if debug or s_dbg.v > 0:
            e.wait_ge(s_dbg.sem, s_dbg.v)
    @block.gpsimd
    def _(e):
        e.wait_ge(s_out.sem, V_OUT)

    pF2.close(); pR3.close(); pE2.close()
    root.close()
    return nc


def _shard_inputs(L0_attn_x, L0_attn_W_q, L0_attn_W_k, L0_attn_W_v, L0_attn_W_o,
                  L0_ffn_w_gate, L0_ffn_w_up, L0_ffn_w_down, L0_c, L0_wmod, tcore):
    f32 = np.float32
    bf16 = ml_dtypes.bfloat16
    x = np.asarray(L0_attn_x, dtype=f32)
    wqT = np.ascontiguousarray(np.asarray(L0_attn_W_q, f32).T[PERM_D, :])
    wkT = np.ascontiguousarray(np.asarray(L0_attn_W_k, f32).T[PERM_D, :])
    wvT = np.ascontiguousarray(np.asarray(L0_attn_W_v, f32).T[PERM_D, :])
    woT = np.ascontiguousarray(np.asarray(L0_attn_W_o, f32).T[:, PERM_D])
    wgT = np.ascontiguousarray(np.asarray(L0_ffn_w_gate, f32).T[PERM_D, :][:, PERM_I])
    wuT = np.ascontiguousarray(np.asarray(L0_ffn_w_up, f32).T[PERM_D, :][:, PERM_I])
    wdT = np.ascontiguousarray(np.asarray(L0_ffn_w_down, f32).T[PERM_I, :][:, PERM_D])
    c = np.asarray(L0_c, f32)
    wmod = np.asarray(L0_wmod, f32).reshape(D, 6, D)[:, :, PERM_D].reshape(D, 6 * D)

    identb = np.eye(128, dtype=bf16)
    identf = np.eye(128, dtype=f32)
    onesb = np.ones((128, 128), dtype=bf16)
    tril = np.tril(np.ones((CHUNK, CHUNK), dtype=f32))
    mask_ji = (np.arange(CHUNK)[:, None] <= np.arange(CHUNK)[None, :]).astype(f32)
    trilb = np.tile(mask_ji, (1, 4)).astype(bf16)
    modadd = np.zeros((128, 64), dtype=f32)
    modadd[:, 8:16] = 1.0
    modadd[:, 32:40] = 1.0
    modmsk = np.zeros((128, 64), dtype=f32)
    modmsk[:, 16:24] = 1.0
    modmsk[:, 40:48] = 1.0
    modinv = (1.0 - modmsk).astype(f32)

    in_maps = []
    for core in range(NCORES):
        b = core // 4
        q = core % 4
        xs = np.ascontiguousarray(x[b, q * tcore:(q + 1) * tcore, :][:, PERM_D].T)
        pm = np.zeros((128, 8), dtype=f32)
        for r in range(NCORES):
            if r // 4 == b and r % 4 < q:
                pm[:, r] = 1.0
        in_maps.append({
            "xin": xs,
            "wq": wqT, "wk": wkT, "wv": wvT, "wo": woT,
            "wg": np.ascontiguousarray(wgT[:, core * 352:(core + 1) * 352]),
            "wu": np.ascontiguousarray(wuT[:, core * 352:(core + 1) * 352]),
            "wd": np.ascontiguousarray(wdT[:, core * 128:(core + 1) * 128]),
            "cvec": np.ascontiguousarray(c[b:b + 1, :]),
            "wmod": np.ascontiguousarray(wmod[:, core % 4 * 1536:(core % 4 + 1) * 1536]),
            "identb": identb, "identf": identf, "onesb": onesb, "trilb": trilb,
            "pmask": pm, "modadd": modadd, "modmsk": modmsk, "modinv": modinv,
        })
    return in_maps


def _kernel_impl(t_full, runner, L0_attn_x, L0_attn_W_q, L0_attn_W_k, L0_attn_W_v,
                 L0_attn_W_o, L0_ffn_w_gate, L0_ffn_w_up, L0_ffn_w_down, L0_c, L0_wmod):
    tcore = t_full // 4
    in_maps = _shard_inputs(L0_attn_x, L0_attn_W_q, L0_attn_W_k, L0_attn_W_v, L0_attn_W_o,
                            L0_ffn_w_gate, L0_ffn_w_up, L0_ffn_w_down, L0_c, L0_wmod, tcore)
    nc = build_program(tcore)
    results = runner(nc, in_maps)
    out = np.zeros((B, t_full, D), dtype=np.float32)
    inv = np.argsort(PERM_D)
    can_exp = np.concatenate([in_maps[0]["identb"], in_maps[0]["onesb"],
                              in_maps[0]["trilb"], np.zeros((128, 512), in_maps[0]["trilb"].dtype)], axis=1)
    for core in range(NCORES):
        b, q = core // 4, core % 4
        can = np.asarray(results[core]["canary"])
        if not np.array_equal(can, can_exp):
            raise RuntimeError(f"core {core}: on-chip constants corrupted "
                               f"(instruction-stream hazard); refusing to return output")
        o = np.asarray(results[core]["out"])
        out[b, q * tcore:(q + 1) * tcore, :] = o.T[:, inv]
    return out


def kernel(L0_attn_x, L0_attn_W_q, L0_attn_W_k, L0_attn_W_v, L0_attn_W_o,
           L0_ffn_w_gate, L0_ffn_w_up, L0_ffn_w_down, L0_c, L0_wmod):
    def runner(nc, in_maps):
        res = run_bass_kernel_spmd(nc, in_maps, core_ids=list(range(NCORES)))
        return res.results
    return _kernel_impl(T_FULL, runner, L0_attn_x, L0_attn_W_q, L0_attn_W_k, L0_attn_W_v,
                        L0_attn_W_o, L0_ffn_w_gate, L0_ffn_w_up, L0_ffn_w_down, L0_c, L0_wmod)
